# version 05 (final)
# baseline (speedup 1.0000x reference)
import numpy as np
import ml_dtypes
from contextlib import ExitStack

import concourse.bass as bass
import concourse.mybir as mybir
from concourse.bass_utils import run_bass_kernel_spmd

F32 = mybir.dt.float32
BF16 = mybir.dt.bfloat16
U32 = mybir.dt.uint32
ALU = mybir.AluOpType
AF = mybir.ActivationFunctionType
AX = mybir.AxisListType
BF = ml_dtypes.bfloat16

NCORES = 8
D = 2048
DH = 8192
TC = 1024
TP = TC + 3
HP = TP + 1
H = 16
C = 64
NCH = TC // C
LN2 = float(np.log(2.0))
GELU_C = 0.7978845608028654
ATT_SCALE = float(128.0 ** -0.5)
MAGIC = 12582912.0

ENG_SEM = {"sync": "qs", "gpsimd": "qg", "tensor": "tt", "vector": "vv",
           "scalar": "ss", "scalq": "qa", "gpsimdd": "qd", "gpsimdc": "qc"}
PHYS = {"sync": "sync", "gpsimd": "gpsimd", "tensor": "tensor",
        "vector": "vector", "scalar": "scalar", "scalq": "scalar",
        "gpsimdd": "gpsimd", "gpsimdc": "gpsimd"}


class Step:
    __slots__ = ("eng", "fn", "deps", "n_inc", "sem", "val", "idx", "is_dma")

    def __init__(self, eng, fn, deps, n_inc, is_dma=False):
        self.eng, self.fn, self.deps, self.n_inc = eng, fn, list(deps), n_inc
        self.is_dma = is_dma


class Plan:
    def __init__(self):
        self.steps = []

    def add(self, eng, fn, deps=(), n_inc=1, is_dma=False):
        st = Step(eng, fn, [d for d in deps if d is not None], n_inc, is_dma)
        st.idx = len(self.steps)
        for d in st.deps:
            assert d.idx < st.idx, "forward dependency"
        self.steps.append(st)
        return st

    def resolve(self):
        cnt = {}
        for st in self.steps:
            sem = ENG_SEM[st.eng]
            cnt[sem] = cnt.get(sem, 0) + st.n_inc
            st.sem, st.val = sem, cnt[sem]

    def emit(self, block, sems):
        bodies = {"sync": block.sync, "gpsimd": block.gpsimd,
                  "tensor": block.tensor, "vector": block.vector,
                  "scalar": block.scalar}

        def _ap_span(ap):
            try:
                name = ap.tensor.name
                off = int(ap.offset)
                end = off
                for pair in ap.ap:
                    st_, cnt = int(pair[0]), int(pair[1])
                    end += (cnt - 1) * abs(st_)
                return (name, off, end + 1)
            except Exception:
                try:
                    return (ap.tensor.name, 0, 1 << 60)
                except Exception:
                    return None

        class _EngProxy:
            def __init__(self, eng, track=True):
                self._e = eng
                self._last = None
                self._track = track
                self._w = []
                self._r = []

            def _overlap(self, span, lst):
                if span is None:
                    return True
                n, a, b = span
                for (n2, a2, b2) in lst:
                    if n == n2 and a < b2 and a2 < b:
                        return True
                return False

            def drain_now(self):
                self._w = []
                self._r = []
                d = self._e.drain()
                self._last = d
                return d

            def __getattr__(self, name):
                a = getattr(self._e, name)
                if not callable(a):
                    return a
                if name in ("nop", "engine_nop"):
                    def call_nop(*args, **kw):
                        return self._last
                    return call_nop

                def call(*args, **kw):
                    if self._track and name not in ("wait_ge", "drain", "memset"):
                        aps = []
                        for x in args:
                            if hasattr(x, "tensor") and hasattr(x, "offset"):
                                aps.append(x)
                        kw_out = kw.get("out")
                        kw_in = kw.get("in_")
                        if name == "dma_start":
                            ispans = [_ap_span(kw_in)] if (kw_in is not None and hasattr(kw_in, "tensor")) else []
                            need = any(self._overlap(s, self._w) for s in ispans)
                            if need:
                                self.drain_now()
                        else:
                            outs = []
                            ins = []
                            if aps:
                                outs.append(aps[0])
                                ins.extend(aps[1:])
                            if kw_out is not None and hasattr(kw_out, "tensor"):
                                outs.append(kw_out)
                            for kname in ("scale", "bias"):
                                kv = kw.get(kname)
                                if kv is not None and hasattr(kv, "tensor"):
                                    ins.append(kv)
                            ospans = [_ap_span(o) for o in outs]
                            ispans = [_ap_span(i) for i in ins]
                            need = (any(self._overlap(s, self._w) for s in ispans)
                                    or any(self._overlap(s, self._w) for s in ospans)
                                    or any(self._overlap(s, self._r) for s in ospans))
                            if need:
                                self.drain_now()
                            for s in ospans:
                                self._w.append(s if s else ("?", 0, 1 << 60))
                            for s in ispans:
                                if s:
                                    self._r.append(s)
                    r = a(*args, **kw)
                    if r is not None and name != "wait_ge":
                        self._last = r
                    return r
                return call

        for ename, deco in bodies.items():
            mine = [st for st in self.steps if PHYS[st.eng] == ename]

            def body(eng, mine=mine, ename=ename):
                track = ename != "tensor"
                px = _EngProxy(eng, track=track)
                seen = {}
                for st in mine:
                    for d in st.deps:
                        if PHYS[d.eng] == PHYS[st.eng] and not d.is_dma:
                            continue
                        if seen.get(d.sem, -1) >= d.val:
                            continue
                        eng.wait_ge(sems[d.sem], d.val)
                        seen[d.sem] = d.val
                    last = st.fn(px)
                    if last is not None:
                        if st.is_dma or not track:
                            last.then_inc(sems[st.sem], st.n_inc)
                        else:
                            px.drain_now().then_inc(sems[st.sem], st.n_inc)
            deco(body)


def build_program(debug=False):
    nc = bass.Bass("TRN2", target_bir_lowering=False, detect_race_conditions=False)
    T = {}

    def inp(name, shape, dt=F32):
        T[name] = nc.dram_tensor(name, shape, dt, kind="ExternalInput").ap()

    def idram(name, shape, dt):
        T[name] = nc.dram_tensor(name, shape, dt, kind="Internal").ap()

    inp("x_in", [TP, D])
    for w in ("Wq", "Wk", "Wv", "Ww", "Wo"):
        inp(w, [D, D])
    inp("cq", [D, 4]); inp("ck", [D, 4]); inp("cv", [D, 4])
    inp("Wf1", [D, 128]); inp("Wf2", [128, D]); inp("Wb", [D, H])
    inp("Wg1", [D, 128]); inp("Wg2", [128, D])
    inp("bg2", [1, D]); inp("norm_w", [1, 128])
    inp("lnw", [1, D]); inp("lnb", [1, D])
    inp("W1", [DH, D]); inp("b1", [1, DH])
    inp("W2", [D, DH]); inp("b2", [1, D])
    inp("idn", [128, 128], BF16)
    inp("ones1", [1, 128], BF16)
    inp("csA", [C, C], BF16)
    inp("csS", [C, C], BF16)
    inp("mask64", [C, C], BF16)
    inp("pfx", [128, 3])
    inp("mask7", [1, 7])
    inp("mask7b", [128, 7])
    T["out"] = nc.dram_tensor("out", [TC, D], F32, kind="ExternalOutput").ap()

    idram("x1b", [TP, D], F32)
    idram("x2b", [TC, D], F32)
    idram("ghb", [TC, D], BF16)
    idram("glb", [TC, D], BF16)
    idram("ktokb", [TC, D], BF16)
    idram("vtokb", [TC, D], BF16)
    idram("bwb", [TC, D], BF16)
    idram("qfb", [128, H * TC], BF16)
    idram("kfb", [128, H * TC], BF16)
    idram("ob", [TC, D], BF16)
    idram("qw1t", [16, D, 512], BF16)
    idram("qw2t", [16, DH, 128], BF16)
    idram("rowsb", [4, DH], BF16)
    CCW = 131072 * 2 + 2048
    idram("cc_in", [CCW], U32)
    idram("cc_out", [NCORES * CCW], U32)
    T["CCW"] = CCW

    es = ExitStack()

    def sbuf(name, shape, dt):
        T[name] = es.enter_context(nc.sbuf_tensor(name, shape, dt))

    def psum(name, shape, dt):
        T[name] = es.enter_context(nc.psum_tensor(name, shape, dt))

    sbuf("idn_s", [128, 128], BF16)
    sbuf("ones_s", [1, 128], BF16)
    sbuf("csA_s", [C, C], BF16)
    sbuf("csS_s", [C, C], BF16)
    sbuf("m64_s", [C, C], BF16)
    sbuf("pfx_s", [128, 3], F32)
    sbuf("m7_s", [1, 7], F32)
    sbuf("m7b_s", [128, 7], F32)
    sbuf("actC", [128, 8], F32)
    sbuf("lnw_bc", [128, D], F32)
    sbuf("lnb_bc", [128, D], F32)
    sbuf("nw_bc", [128, 128], F32)
    sbuf("rowslot", [1, 1024], BF16)
    sbuf("SkS", [128, D], BF16)
    sbuf("SvS", [128, D], BF16)
    sbuf("beta_s", [128, 8 * H], F32)
    sbuf("tinyA", [128, 64], F32)
    sbuf("tinyB", [128, 64], F32)
    sbuf("tinyC", [128, 64], F32)
    sbuf("tinyD", [128, 64], F32)
    sbuf("dvm", [128, 16], BF16)
    sbuf("tokst", [128, 1024], BF16)
    sbuf("arA", [128, 16448], BF16)
    sbuf("arB", [128, 8256], BF16)
    sbuf("arC", [128, 32960], BF16)
    sbuf("arS", [128, 8448], BF16)
    sbuf("fx", [128, 1028], F32)
    sbuf("fx2", [128, D], F32)
    sbuf("fsq", [128, D], F32)
    sbuf("fhb", [128, D], BF16)
    sbuf("qh", [128, D], BF16)
    sbuf("sm1", [128, 512], BF16)
    sbuf("sm2", [128, 512], BF16)

    psum("psA", [128, 2048], F32)
    psum("psB", [128, 1024], F32)
    psum("psT1", [128, 1024], BF16)
    psum("psT2", [128, 1024], BF16)

    P = Plan()
    with es:
        ses = ExitStack()
        sems = {}
        with ses:
            for s in set(ENG_SEM.values()):
                sems[s] = ses.enter_context(nc.semaphore(s))
            _plan(P, nc, T, sems)
            P.resolve()
            with nc.Block() as block:
                P.emit(block, sems)
    return nc


def _plan(P, nc, T, sems):
    idn = T["idn_s"]; ones1 = T["ones_s"]
    arA, arB, arC, arS = T["arA"], T["arB"], T["arC"], T["arS"]
    arBf = arB.ap().bitcast(F32)
    arAf = arA.ap().bitcast(F32)
    arCf = arC.ap().bitcast(F32)

    Wf2b = arB.ap()[:, 0:2048]
    Wg2b = arB.ap()[:, 2048:4096]
    Wbb = arB.ap()[:, 4096:4352]
    cqs = arBf[:, 2176:2240]
    cks = arBf[:, 2240:2304]
    cvs = arBf[:, 2304:2368]
    hf1 = arB.ap()[:, 4736:5764]
    hg1 = arB.ap()[:, 5764:6792]

    dbc = arA.ap()[:, 0:2048]
    drow = arA.ap()[0:1, 2048:4096]
    srow = arAf[0:1, 2048:4096]
    lacc = arAf[0:16, 4224:4352]
    lsc = arAf[0:16, 4352:4480]

    O_GHI, O_GLO, O_KT, O_VT, O_BW = 0, 2048, 4096, 6144, 8192
    O_U, O_VU, O_EXPA = 10240, 12288, 14336
    O_QF, O_KF, O_QKB, O_ATB, O_UTB = 16384, 17408, 18432, 19456, 20480
    O_OEV = 21504
    O_SCF = 23552
    O_BJ, O_BJV = 23552, 25600
    O_ACHT = 27648
    scf = arCf[0:64, 11776:13824]

    def a_ap(off, p0, p1, w):
        return arC.ap()[p0:p1, off:off + w]

    def dma(eng, out, in_, deps=()):
        q = ENG_SEM[eng]

        def fn(e, out=out, in_=in_):
            e.dma_start(out=out, in_=in_).then_inc(sems[q], 16)
            return None
        return P.add(eng, fn, deps, n_inc=16, is_dma=True)

    def mdma(eng, pairs, deps=()):
        q = ENG_SEM[eng]

        def fn(e, pairs=pairs):
            for out, in_ in pairs:
                e.dma_start(out=out, in_=in_).then_inc(sems[q], 16)
            return None
        return P.add(eng, fn, deps, n_inc=16 * len(pairs), is_dma=True)

    ACT = T["actC"].ap()
    A_LN2, A_NLN2 = ACT[:, 0:1], ACT[:, 1:2]
    A_6LN2, A_N6LN2 = ACT[:, 2:3], ACT[:, 3:4]

    ld0 = mdma("sync", [
        (T["idn_s"][:], T["idn"]), (T["ones_s"][:], T["ones1"]),
        (T["csA_s"][:], T["csA"]), (T["csS_s"][:], T["csS"]),
        (T["m64_s"][:], T["mask64"]), (T["pfx_s"][:], T["pfx"]),
        (T["m7_s"][:], T["mask7"]),
        (T["m7b_s"][:], T["mask7b"]),
    ])

    def f_init(e):
        for i, val in enumerate([LN2, -LN2, 6 * LN2, -6 * LN2, GELU_C]):
            e.memset(ACT[:, i:i + 1], float(val))
        e.memset(T["SkS"][:], 0.0)
        return e.memset(T["SvS"][:], 0.0)
    st_ac = P.add("gpsimd", f_init, (), 1)

    split_jobs = []
    hilo_steps = []
    prev = ld0
    fxr = T["fx"].ap()[0:1, 0:1024]

    def emit_split(src_hbm, w_total, dst_hi, bcast_dst, deps):
        outs = []
        last = deps
        for off in range(0, w_total, 512):
            ldp = dma("sync", T["fx"].ap()[0:1, 0:512], src_hbm[0:1, off:off + 512],
                      last)

            def fsp(e):
                e.tensor_copy(T["rowslot"][0:1, 0:512], T["fx"].ap()[0:1, 0:512])
                return e.tensor_tensor(T["rowslot"][0:1, 512:1024],
                                       T["fx"].ap()[0:1, 0:512],
                                       T["rowslot"][0:1, 0:512], ALU.subtract)
            ssp = P.add("vector", fsp, (ldp,), 1)
            r, cc0 = dst_hi
            sth = mdma("scalq", [
                (T["rowsb"][r:r + 1, cc0 + off:cc0 + off + 512],
                 T["rowslot"][0:1, 0:512]),
                (T["rowsb"][r + 1:r + 2, cc0 + off:cc0 + off + 512],
                 T["rowslot"][0:1, 512:1024]),
            ], (ssp,))
            if bcast_dst is not None:
                def fbc(e, off=off):
                    e.matmul(T["psB"].ap()[:, 0:512], ones1[:],
                             T["rowslot"][0:1, 0:512], start=True, stop=False)
                    return e.matmul(T["psB"].ap()[:, 0:512], ones1[:],
                                    T["rowslot"][0:1, 512:1024],
                                    start=False, stop=True)
                sb1 = P.add("tensor", fbc, (ssp,) + tuple(outs[-1:]), 1)

                def fev(e, off=off, bcast_dst=bcast_dst):
                    return e.tensor_copy(bcast_dst[:, off:off + 512],
                                         T["psB"].ap()[:, 0:512])
                sev = P.add("vector", fev, (sb1,), 1)
                outs.append(sev)
                last = (sth, sev)
            else:
                last = (sth,)
                outs.append(sth)
        return outs[-1], outs

    s_b1, _ = emit_split(T["b1"], DH, (0, 0), None, (ld0,))
    s_b2, _ = emit_split(T["b2"], D, (2, 0), None, (s_b1,))
    s_bg, _ = emit_split(T["bg2"], D, (2, 2048), None, (s_b2,))
    s_lnw, _ = emit_split(T["lnw"], D, (2, 4096), T["lnw_bc"].ap(), (s_bg,))
    s_lnb, _ = emit_split(T["lnb"], D, (2, 6144), T["lnb_bc"].ap(), (s_lnw,))

    ld_nw = dma("sync", T["fx"].ap()[0:1, 0:128], T["norm_w"], (s_lnb,))

    def f_nw(e):
        e.tensor_copy(T["rowslot"][0:1, 0:128], T["fx"].ap()[0:1, 0:128])
        return e.tensor_tensor(T["rowslot"][0:1, 512:640],
                               T["fx"].ap()[0:1, 0:128],
                               T["rowslot"][0:1, 0:128], ALU.subtract)
    s_nws = P.add("vector", f_nw, (ld_nw,), 1)

    def f_nwb(e):
        e.matmul(T["psB"].ap()[:, 0:128], ones1[:], T["rowslot"][0:1, 0:128],
                 start=True, stop=False)
        return e.matmul(T["psB"].ap()[:, 0:128], ones1[:],
                        T["rowslot"][0:1, 512:640], start=False, stop=True)
    s_nwm = P.add("tensor", f_nwb, (s_nws, s_lnb), 1)
    s_nwe = P.add("vector",
                  lambda e: e.tensor_copy(T["nw_bc"].ap()[:],
                                          T["psB"].ap()[:, 0:128]), (s_nwm,), 1)
    consts_done = s_nwe

    def quant(src_ap, dst_ap, ysc, p, width, deps):
        nb = width // 32
        tA = T["tinyA"].ap()[0:p, 0:nb]
        tB = T["tinyB"].ap()[0:p, 0:nb]
        tC = T["tinyC"].ap()[0:p, 0:nb]
        tD = T["tinyD"].ap()[0:p, 0:nb]
        s3 = src_ap.rearrange("p (b k) -> p b k", k=32)

        def f1(e):
            e.tensor_reduce(tA, s3, AX.X, ALU.max, apply_absolute_value=True)
            return e.tensor_scalar(tA, tA, float(2.0 ** -20), None, ALU.max)
        s_am = P.add("vector", f1, deps, 1)
        s_ln = P.add("scalar", lambda e: e.activation(tB, tA, AF.Ln), (s_am,), 1)

        def f3(e):
            e.tensor_scalar(tB, tB, 1.0 / LN2, -0.4990, ALU.mult, ALU.add)
            return e.tensor_scalar(tB, tB, MAGIC, MAGIC, ALU.add, ALU.subtract)
        s_e = P.add("vector", f3, (s_ln,), 1)

        def f4(e):
            e.activation(tC, tB, AF.Exp, scale=A_NLN2[0:p, :], bias=A_6LN2[0:p, :])
            return e.activation(tD, tB, AF.Exp, scale=A_LN2[0:p, :],
                                bias=A_N6LN2[0:p, :])
        s_x = P.add("scalar", f4, (s_e,), 1)
        y3 = ysc.rearrange("p (b k) -> p b k", k=32)

        def f5(e):
            e.tensor_tensor(y3, s3, tC.unsqueeze(2).broadcast_to([p, nb, 32]),
                            ALU.mult)
            e.tensor_scalar(ysc, ysc, MAGIC, MAGIC, ALU.add, ALU.subtract)
            return e.scalar_tensor_tensor(
                dst_ap.rearrange("p (b k) -> p b k", k=32), y3, 127.0,
                tD.unsqueeze(2).broadcast_to([p, nb, 32]), ALU.min, ALU.mult)
        return P.add("vector", f5, (s_x,), 1)

    w2f = arCf
    prev_ev = [None, None]
    prev_q = None
    prev_smt_slab = None
    last_store = None
    w1_done = [None] * 16
    for kind, nslab in (("w1", 64), ("w2", 16)):
        for i in range(nslab):
            if kind == "w1":
                src = T["W1"][i * 128:(i + 1) * 128, :]
                buf, qbuf, width = T["fsq"].ap(), T["qh"].ap(), D
            else:
                src = T["W2"][i * 128:(i + 1) * 128, :]
                buf, qbuf, width = w2f[:, 0:DH], arB.ap()[:, 0:DH], DH
            ld = dma("sync", buf[:, 0:width], src,
                     ((prev_q, prev_smt_slab) if prev_smt_slab else (prev_q,))
                     if prev_q else (consts_done, st_ac))
            if kind == "w2":
                qs_parts = []
                for part in range(4):
                    sl = slice(part * D, (part + 1) * D)
                    qs_parts.append(quant(buf[:, sl], qbuf[:, sl],
                                          T["fx2"].ap(), 128, D,
                                          (ld,) if part == 0 else (ld, qs_parts[-1])))
                s_q = qs_parts[-1]
            else:
                s_q = quant(buf[:, 0:width], qbuf[:, 0:width], T["fx2"].ap(),
                            128, width, (ld,))
            prev_q = s_q
            ntile = width // 128
            for gidx in range(ntile // 4):
                pst = T["psT1"] if gidx % 2 == 0 else T["psT2"]
                anti = prev_ev[gidx % 2]

                def fmt(e, gidx=gidx, pst=pst, qbuf=qbuf):
                    for k in range(4):
                        t0 = (gidx * 4 + k) * 128
                        e.transpose(pst.ap()[:, k * 128:(k + 1) * 128],
                                    qbuf[:, t0:t0 + 128], idn[:])
                    return e.nop()
                smt = P.add("tensor", fmt, (s_q,) + ((anti,) if anti else ()), 1)
                prev_smt_slab = smt

                def fev(e, pst=pst):
                    return e.tensor_copy(T["tokst"].ap()[:, 0:512], pst.ap()[:, 0:512])
                dep_ev = [smt] + ([last_store] if last_store else [])
                sev = P.add("vector", fev, dep_ev, 1)
                prev_ev[gidx % 2] = sev
                if kind == "w1":
                    dst = T["qw1t"][i // 4, gidx * 512:(gidx + 1) * 512,
                                    (i % 4) * 128:(i % 4) * 128 + 128]
                else:
                    dst = T["qw2t"][i, gidx * 512:(gidx + 1) * 512, :]
                dst = dst.rearrange("(c p) o -> p c o", p=128)
                srcap = T["tokst"].ap()[:, 0:512].rearrange("p (c o) -> p c o", o=128)
                last_store = dma("scalq", dst, srcap, (sev,))
                if kind == "w1":
                    w1_done[i // 4] = last_store
    wprep_done = last_store
    w2_done = last_store

    def ln_steps(xt_ap, p, deps, out_fn, tiny):
        tX = T[tiny].ap()[0:p, :]
        s1 = P.add("vector",
                   lambda e: e.tensor_reduce(tX[:, 0:1], xt_ap, AX.X, ALU.add),
                   deps, 1)
        s2 = P.add("scalar",
                   lambda e: e.activation(T["fsq"].ap()[0:p, :], xt_ap, AF.Square),
                   deps, 1)

        def f_var(e):
            e.tensor_reduce(tX[:, 1:2], T["fsq"].ap()[0:p, :], AX.X, ALU.add)
            e.tensor_scalar(tX[:, 0:1], tX[:, 0:1], 1.0 / D, None, ALU.mult)
            e.tensor_scalar(tX[:, 1:2], tX[:, 1:2], 1.0 / D, 1e-5, ALU.mult, ALU.add)
            e.tensor_tensor(tX[:, 2:3], tX[:, 0:1], tX[:, 0:1], ALU.mult)
            return e.tensor_tensor(tX[:, 1:2], tX[:, 1:2], tX[:, 2:3], ALU.subtract)
        s3 = P.add("vector", f_var, (s1, s2), 1)
        s4 = P.add("scalar",
                   lambda e: e.activation(tX[:, 3:4], tX[:, 1:2], AF.Sqrt),
                   (s3,), 1)

        def f_fin(e):
            e.reciprocal(tX[:, 4:5], tX[:, 3:4])
            return out_fn(e, tX[:, 0:1], tX[:, 4:5])
        return P.add("vector", f_fin, (s4,), 1)

    def ffn(which, deps_in):
        src = T["x_in"] if which == 1 else T["x2b"]
        dst = T["x1b"] if which == 1 else T["out"]
        if which == 1:
            tiles = [(i * 128, 128) for i in range(8)] + [(1024, 3)]
            halves = [tiles[0:4], tiles[4:9]]
        else:
            tiles = [(i * 128, 128) for i in range(8)]
            halves = [tiles[0:4], tiles[4:8]]
        last = list(deps_in)
        outs = []
        for htiles in halves:
            nrows = sum(p for _, p in htiles)
            colof = {}
            c0 = 0
            for (r0, p) in htiles:
                colof[r0] = c0
                c0 += p
            evsA = {}
            lastq = None
            last_smtA = None
            prev_tr = [None, None]
            tidx = 0
            for (r0, p) in htiles:
                co = colof[r0]
                xt = T["fx2"].ap()[0:p, :]
                ldx = dma("sync", xt, src[r0:r0 + p, :],
                          tuple(last) + ((lastq,) if lastq else ())
                          + ((last_smtA,) if last_smtA else ()))
                xlnd = arA.ap()[0:p, tidx * D:(tidx + 1) * D]

                def mk1(xt=xt, xlnd=xlnd):
                    def out1(e, mu, rstd):
                        return e.tensor_scalar(xlnd, xt, mu, rstd,
                                               ALU.subtract, ALU.mult)
                    return out1
                s_xln = ln_steps(xt, p, (ldx,), mk1(), "tinyA")

                s_sq2 = P.add("scalar",
                              lambda e, p=p, xlnd=xlnd: e.activation(
                                  T["fsq"].ap()[0:p, :], xlnd, AF.Square),
                              (s_xln,), 1)
                s_s1b = P.add("vector",
                              lambda e, p=p, xlnd=xlnd: e.tensor_reduce(
                                  T["tinyB"].ap()[0:p, 0:1], xlnd, AX.X, ALU.add),
                              (s_xln,), 1)

                def f_var2(e, p=p):
                    tB = T["tinyB"].ap()[0:p, :]
                    e.tensor_reduce(tB[:, 1:2], T["fsq"].ap()[0:p, :], AX.X, ALU.add)
                    e.tensor_scalar(tB[:, 0:1], tB[:, 0:1], 1.0 / D, None, ALU.mult)
                    e.tensor_scalar(tB[:, 1:2], tB[:, 1:2], 1.0 / D, 1e-5,
                                    ALU.mult, ALU.add)
                    e.tensor_tensor(tB[:, 2:3], tB[:, 0:1], tB[:, 0:1], ALU.mult)
                    return e.tensor_tensor(tB[:, 1:2], tB[:, 1:2], tB[:, 2:3],
                                           ALU.subtract)
                s_v2 = P.add("vector", f_var2, (s_sq2, s_s1b), 1)
                s_r2 = P.add("scalar",
                             lambda e, p=p: e.activation(
                                 T["tinyB"].ap()[0:p, 3:4],
                                 T["tinyB"].ap()[0:p, 1:2], AF.Sqrt), (s_v2,), 1)

                def f_h(e, p=p, xlnd=xlnd):
                    tB = T["tinyB"].ap()[0:p, :]
                    e.reciprocal(tB[:, 4:5], tB[:, 3:4])
                    e.scalar_tensor_tensor(T["fsq"].ap()[0:p, :], xlnd, tB[:, 0:1],
                                           T["lnw_bc"].ap()[0:p, :],
                                           ALU.subtract, ALU.mult)
                    return e.scalar_tensor_tensor(
                        T["fhb"].ap()[0:p, :], T["fsq"].ap()[0:p, :], tB[:, 4:5],
                        T["lnb_bc"].ap()[0:p, :], ALU.mult, ALU.add)
                s_h = P.add("vector", f_h, (s_r2,), 1)
                s_q = quant(T["fhb"].ap()[0:p, :], T["qh"].ap()[0:p, :],
                            T["fsq"].ap()[0:p, :], p, D, (s_h,))
                lastq = s_q
                for gidx in range(4):
                    pst = T["psT1"] if gidx % 2 == 0 else T["psT2"]
                    anti = prev_tr[gidx % 2]

                    def fmt(e, gidx=gidx, pst=pst, p=p):
                        for k in range(4):
                            kcc = gidx * 4 + k
                            e.transpose(pst.ap()[:, k * 128:k * 128 + p],
                                        T["qh"].ap()[0:p, kcc * 128:(kcc + 1) * 128],
                                        idn[0:p, 0:p])
                        return e.nop()
                    smt = P.add("tensor", fmt, (s_q,) + ((anti,) if anti else ()), 1)
                    last_smtA = smt

                    def fev(e, gidx=gidx, pst=pst, p=p, co=co, nrows=nrows):
                        for k in range(4):
                            kcc = gidx * 4 + k
                            e.tensor_copy(
                                arB.ap()[:, kcc * nrows + co:kcc * nrows + co + p],
                                pst.ap()[:, k * 128:k * 128 + p])
                        return e.nop()
                    sev = P.add("vector", fev, (smt,), 1)
                    prev_tr[gidx % 2] = sev
                    evsA.setdefault(r0, []).append(sev)
                tidx += 1

            prev_slab_mm = [None, None]
            psq = [None] * 4
            prev_q1ev = [None, None]
            prev_brow = [None, None]
            q1evs = []
            lastg = {}
            for s in range(32):
                par = s % 2
                slot = arS.ap()[:, par * 4096:par * 4096 + 4096].rearrange(
                    "p (c o) -> p c o", o=256)
                dep_ld = [prev_slab_mm[par]] if prev_slab_mm[par] else [w1_done[s // 2]]
                ldw = dma("sync", slot,
                          T["qw1t"][s // 2, :, (s % 2) * 256:(s % 2) * 256 + 256]
                          .rearrange("(c p) o -> p c o", p=128), dep_ld)
                rs_hi = T["rowslot"].ap()[0:1, par * 512:par * 512 + 256]
                rs_lo = T["rowslot"].ap()[0:1, par * 512 + 256:par * 512 + 512]
                ldb = mdma("sync", [
                    (rs_hi, T["rowsb"][0:1, s * 256:(s + 1) * 256]),
                    (rs_lo, T["rowsb"][1:2, s * 256:(s + 1) * 256]),
                ], (prev_brow[par],) if prev_brow[par] else (s_b1,))
                mmlast = None
                for ti, (r0, p) in enumerate(htiles):
                    co = colof[r0]
                    pq = (s * len(htiles) + ti) % 4
                    psreg = T["psA"].ap()[0:p, pq * 512:pq * 512 + 256]
                    deps = [ldw, ldb] + evsA[r0]
                    if psq[pq] is not None:
                        deps.append(psq[pq])

                    def fmm(e, p=p, co=co, psreg=psreg, slot=slot,
                            rs_hi=rs_hi, rs_lo=rs_lo, nrows=nrows):
                        for kcc in range(16):
                            e.matmul(psreg,
                                     arB.ap()[:, kcc * nrows + co:kcc * nrows + co + p],
                                     slot[:, kcc, :], start=(kcc == 0), stop=False)
                        e.matmul(psreg, ones1[0:1, 0:p], rs_hi,
                                 start=False, stop=False)
                        return e.matmul(psreg, ones1[0:1, 0:p], rs_lo,
                                        start=False, stop=True)
                    smm = P.add("tensor", fmm, deps, 1)
                    mmlast = smm
                    gb = T["sm1"] if ti % 2 == 0 else T["sm2"]
                    dep_g = [smm]
                    if lastg.get(ti % 2) is not None:
                        dep_g.append(lastg[ti % 2])
                    sg = P.add("scalar",
                               lambda e, p=p, psreg=psreg, gb=gb: e.activation(
                                   gb.ap()[0:p, 0:256], psreg, AF.Gelu_apprx_tanh),
                               dep_g, 1)
                    psq[pq] = sg
                    sq2 = quant(gb.ap()[0:p, 0:256], gb.ap()[0:p, 256:512],
                                T["fx"].ap()[0:p, 512:768], p, 256, (sg,))
                    lastg[ti % 2] = sq2
                    g2 = (s * len(htiles) + ti) % 2
                    pst = T["psT1"] if g2 == 0 else T["psT2"]
                    anti = prev_q1ev[g2]

                    def fmt(e, pst=pst, gb=gb, p=p):
                        for k in range(2):
                            e.transpose(pst.ap()[:, k * 128:k * 128 + p],
                                        gb.ap()[0:p, 256 + k * 128:256 + (k + 1) * 128],
                                        idn[0:p, 0:p])
                        return e.nop()
                    smt = P.add("tensor", fmt, (sq2,) + ((anti,) if anti else ()), 1)

                    def fev(e, pst=pst, p=p, co=co, s=s, nrows=nrows):
                        for k in range(2):
                            occ = s * 2 + k
                            e.tensor_copy(
                                arC.ap()[:, occ * nrows + co:occ * nrows + co + p],
                                pst.ap()[:, k * 128:k * 128 + p])
                        return e.nop()
                    sev = P.add("vector", fev, (smt,), 1)
                    prev_q1ev[g2] = sev
                    q1evs.append(sev)
                prev_slab_mm[par] = mmlast
                prev_brow[par] = mmlast

            prev_smm = None
            psb_last = [None, None]
            prev_xo = None
            prev_brow2 = None
            prev_sst = [None, None]
            houts = []
            for r in range(16):
                slot2 = arS.ap()[:, 0:8192].rearrange("p (c o) -> p c o", o=128)
                ldw2 = dma("sync", slot2,
                           T["qw2t"][r].rearrange("(c p) o -> p c o", p=128),
                           (prev_smm,) if prev_smm else ([w2_done] + q1evs[-2:]))
                rs_hi = T["rowslot"].ap()[0:1, 0:128]
                rs_lo = T["rowslot"].ap()[0:1, 128:256]
                ldb2 = mdma("sync", [
                    (rs_hi, T["rowsb"][2:3, r * 128:(r + 1) * 128]),
                    (rs_lo, T["rowsb"][3:4, r * 128:(r + 1) * 128]),
                ], (prev_brow2,) if prev_brow2 else (s_b2,) + tuple(q1evs[-1:]))
                mm2last = None
                for ti, (r0, p) in enumerate(htiles):
                    co = colof[r0]
                    par = ti % 2
                    psreg = T["psB"].ap()[0:p, par * 512:par * 512 + 128]
                    deps = [ldw2, ldb2] + q1evs
                    if psb_last[par] is not None:
                        deps.append(psb_last[par])

                    def fmm(e, p=p, co=co, psreg=psreg, slot2=slot2,
                            rs_hi=rs_hi, rs_lo=rs_lo, nrows=nrows):
                        for occ in range(64):
                            e.matmul(psreg,
                                     arC.ap()[:, occ * nrows + co:occ * nrows + co + p],
                                     slot2[:, occ, :], start=(occ == 0), stop=False)
                        e.matmul(psreg, ones1[0:1, 0:p], rs_hi,
                                 start=False, stop=False)
                        return e.matmul(psreg, ones1[0:1, 0:p], rs_lo,
                                        start=False, stop=True)
                    smm = P.add("tensor", fmm, deps, 1)
                    mm2last = smm
                    y2b = T["sm2"].ap()[0:p, par * 128:par * 128 + 128]
                    sy2 = P.add("scalar",
                                lambda e, psreg=psreg, y2b=y2b: e.activation(
                                    y2b, psreg, AF.Copy), (smm,), 1)
                    psb_last[par] = sy2
                    xr = T["fx"].ap()[0:p, par * 128:par * 128 + 128]
                    ldxr = dma("sync", xr, src[r0:r0 + p, r * 128:(r + 1) * 128],
                               (prev_xo,) if prev_xo else (q1evs[-1],))
                    xo = T["fx"].ap()[0:p, 256 + par * 128:256 + par * 128 + 128]

                    def fxo(e, p=p, co=co, r=r, y2b=y2b, xr=xr, xo=xo, par=par):
                        tidx2 = co // 128
                        xlnsl = arA.ap()[0:p, tidx2 * D + r * 128:
                                         tidx2 * D + (r + 1) * 128]
                        fb = T["sm1"].ap()[0:p, 256 + par * 128:256 + par * 128 + 128]
                        e.tensor_tensor(fb, y2b, xlnsl, ALU.add)
                        return e.scalar_tensor_tensor(xo, fb, 0.5, xr,
                                                      ALU.mult, ALU.add)
                    sxo = P.add("vector", fxo,
                                (sy2, ldxr) + ((prev_sst[par],)
                                               if prev_sst[par] else ()), 1)
                    prev_xo = sxo
                    sst = dma("scalq", dst[r0:r0 + p, r * 128:(r + 1) * 128], xo,
                              (sxo,))
                    prev_sst[par] = sst
                    houts.append(sst)
                prev_smm = mm2last
                prev_brow2 = mm2last
            last = houts[-2:]
            outs.extend(houts)
        return outs

    ffn1_outs = ffn(1, (consts_done, wprep_done))

    hf_evs = []
    prev_tr = [None, None]
    lastx = None
    last_smtH = None
    for (r0, p) in [(i * 128, 128) for i in range(8)] + [(1024, 3)]:
        ldx = dma("sync", T["fx2"].ap()[0:p, :], T["x1b"][r0:r0 + p, :],
                  tuple(ffn1_outs[-2:]) + ((lastx,) if lastx else ())
                  + ((last_smtH,) if last_smtH else ()))

        def mk2(p=p):
            def outf(e, mu, rstd):
                return e.tensor_scalar(T["fhb"].ap()[0:p, :],
                                       T["fx2"].ap()[0:p, :], mu, rstd,
                                       ALU.subtract, ALU.mult)
            return outf
        s_xln = ln_steps(T["fx2"].ap()[0:p, :], p, (ldx,), mk2(), "tinyA")
        lastx = s_xln
        for gidx in range(4):
            pst = T["psT1"] if gidx % 2 == 0 else T["psT2"]
            anti = prev_tr[gidx % 2]

            def fmt(e, gidx=gidx, pst=pst, p=p):
                for k in range(4):
                    kcc = gidx * 4 + k
                    e.transpose(pst.ap()[:, k * 128:k * 128 + p],
                                T["fhb"].ap()[0:p, kcc * 128:(kcc + 1) * 128],
                                idn[0:p, 0:p])
                return e.nop()
            smt = P.add("tensor", fmt, (s_xln,) + ((anti,) if anti else ()), 1)
            last_smtH = smt

            def fev(e, gidx=gidx, pst=pst, p=p, r0=r0):
                for k in range(4):
                    kcc = gidx * 4 + k
                    e.tensor_copy(arA.ap()[:, kcc * HP + r0:kcc * HP + r0 + p],
                                  pst.ap()[:, k * 128:k * 128 + p])
                return e.nop()
            sev = P.add("vector", fev, (smt,), 1)
            prev_tr[gidx % 2] = sev
            hf_evs.append(sev)
    hfeat_done = hf_evs[-8:]

    ldWsm = mdma("gpsimdd", [
        (Wbb.rearrange("p (c h) -> p c h", h=16),
         T["Wb"].rearrange("(c p) h -> p c h", p=128)),
        (Wf2b, T["Wf2"]),
        (Wg2b, T["Wg2"]),
        (cqs.rearrange("p (c f) -> p c f", f=4),
         T["cq"].rearrange("(c p) f -> p c f", p=128)),
        (cks.rearrange("p (c f) -> p c f", f=4),
         T["ck"].rearrange("(c p) f -> p c f", p=128)),
        (cvs.rearrange("p (c f) -> p c f", f=4),
         T["cv"].rearrange("(c p) f -> p c f", p=128)),
    ], tuple(ffn1_outs[-2:]) + tuple(hf_evs[-1:]))

    beta_steps = []
    prev = None
    for tt in range(8):
        def fmm(e, tt=tt):
            for kcc in range(16):
                e.matmul(T["psB"].ap()[:, 0:16],
                         arA.ap()[:, kcc * HP + 3 + tt * 128:kcc * HP + 3 + (tt + 1) * 128],
                         Wbb[:, kcc * 16:(kcc + 1) * 16],
                         start=(kcc == 0), stop=(kcc == 15))
            return e.nop()
        smm = P.add("tensor", fmm, tuple(hfeat_done) + (ldWsm,) +
                    ((prev,) if prev else ()), 1)
        ssg = P.add("scalar",
                    lambda e, tt=tt: e.activation(
                        T["beta_s"].ap()[:, tt * 16:(tt + 1) * 16],
                        T["psB"].ap()[:, 0:16], AF.Sigmoid), (smm,), 1)
        beta_steps.append(ssg)
        prev = ssg

    lr_steps = []
    prev = beta_steps[-1]
    for (W1n, dst_lr) in (("Wf1", hf1), ("Wg1", hg1)):
        ldw = dma("gpsimdd", T["fx2"].ap().bitcast(BF16)[:, 0:2048].rearrange(
            "p (c f) -> p c f", f=128),
            T[W1n].rearrange("(c p) f -> p c f", p=128), (prev,))

        def fmm(e):
            wv = T["fx2"].ap().bitcast(BF16)[:, 0:2048].rearrange(
                "p (c f) -> p c f", f=128)
            for kcc in range(16):
                e.matmul(T["psA"].ap()[:, 0:512], wv[:, kcc, :],
                         arA.ap()[:, kcc * HP:kcc * HP + 512],
                         start=(kcc == 0), stop=(kcc == 15))
                e.matmul(T["psA"].ap()[:, 512:1024], wv[:, kcc, :],
                         arA.ap()[:, kcc * HP + 512:kcc * HP + 1024],
                         start=(kcc == 0), stop=(kcc == 15))
                e.matmul(T["psB"].ap()[:, 0:3], wv[:, kcc, :],
                         arA.ap()[:, kcc * HP + 1024:kcc * HP + 1027],
                         start=(kcc == 0), stop=(kcc == 15))
            return e.nop()
        smm = P.add("tensor", fmm, (ldw,) + tuple(hfeat_done) + (prev,), 1)

        def fev(e, dst_lr=dst_lr):
            e.tensor_copy(dst_lr[:, 0:1024], T["psA"].ap()[:, 0:1024])
            return e.tensor_copy(dst_lr[:, 1024:1027], T["psB"].ap()[:, 0:3])
        sev = P.add("vector", fev, (smm,), 1)
        lr_steps.append(sev)
        prev = sev

    g_steps = []
    prevg = lr_steps[1]
    for tt in range(8):
        def fmm(e, tt=tt):
            for nr in range(4):
                e.matmul(T["psA"].ap()[:, nr * 512:(nr + 1) * 512],
                         hf1[:, 3 + tt * 128:3 + (tt + 1) * 128],
                         Wf2b[:, nr * 512:(nr + 1) * 512],
                         start=True, stop=True)
            return e.nop()
        smm = P.add("tensor", fmm, (lr_steps[0], ldWsm, prevg), 1)
        ssig = P.add("scalar",
                     lambda e: e.activation(T["fsq"].ap()[:], T["psA"].ap()[:],
                                            AF.Sigmoid), (smm,), 1)
        sln = P.add("scalar",
                    lambda e: e.activation(T["fsq"].ap()[:], T["fsq"].ap()[:],
                                           AF.Ln), (ssig,), 1)

        def fsplit(e):
            e.tensor_copy(T["qh"].ap()[:], T["fsq"].ap()[:])
            return e.tensor_tensor(T["fhb"].ap()[:], T["fsq"].ap()[:],
                                   T["qh"].ap()[:], ALU.subtract)
        ssp = P.add("vector", fsplit, (sln,), 1)
        st1 = dma("scalq", T["ghb"][tt * 128:(tt + 1) * 128, :], T["qh"].ap()[:],
                  (ssp,))
        st2 = dma("scalq", T["glb"][tt * 128:(tt + 1) * 128, :], T["fhb"].ap()[:],
                  (ssp,))
        prevg = st2
        g_steps.extend([st1, st2])

    projs = [("Wq", cqs, "q"), ("Wk", cks, "k"), ("Wv", cvs, "v"), ("Ww", cvs, "w")]
    prev_wld = [None, None]
    prev_psA = [None, None]
    prev_tr = [None, None]
    prev_use = g_steps[-1]
    proj_sp = []
    for (Wn, cn, nm) in projs:
        for oc in range(16):
            par = oc % 2
            wslot = arS.ap()[:, par * 2048:par * 2048 + 2048].rearrange(
                "p (c o) -> p c o", o=128)
            lw = dma("gpsimdd", wslot,
                     T[Wn].rearrange("(c p) o -> p c o", p=128)[:, :, oc * 128:(oc + 1) * 128],
                     (prev_wld[par],) if prev_wld[par] else (g_steps[-1],))
            deps = list(hfeat_done) + [lw]
            if prev_psA[par] is not None:
                deps.append(prev_psA[par])
            psreg = T["psA"].ap()[:, par * 1024:par * 1024 + 1024]
            psreg2 = T["psB"].ap()[:, par * 512:par * 512 + 3]

            def fmm(e, wslot=wslot, psreg=psreg, psreg2=psreg2):
                for kcc in range(16):
                    lhs = wslot[:, kcc, :]
                    e.matmul(psreg[:, 0:512], lhs,
                             arA.ap()[:, kcc * HP:kcc * HP + 512],
                             start=(kcc == 0), stop=(kcc == 15))
                    e.matmul(psreg[:, 512:1024], lhs,
                             arA.ap()[:, kcc * HP + 512:kcc * HP + 1024],
                             start=(kcc == 0), stop=(kcc == 15))
                    e.matmul(psreg2, lhs,
                             arA.ap()[:, kcc * HP + 1024:kcc * HP + 1027],
                             start=(kcc == 0), stop=(kcc == 15))
                return e.nop()
            smm = P.add("tensor", fmm, deps, 1)
            prev_wld[par] = smm

            def fev(e, psreg=psreg, psreg2=psreg2):
                e.tensor_copy(T["fx"].ap()[:, 0:1024], psreg[:, 0:1024])
                e.tensor_copy(T["fx"].ap()[:, 1024:1027], psreg2)
                return e.tensor_tensor(T["fx"].ap()[:, 0:3], T["fx"].ap()[:, 0:3],
                                       T["pfx_s"].ap()[:], ALU.mult)
            sev = P.add("vector", fev, (smm, prev_use), 1)
            prev_psA[par] = sev

            def fconv(e, cn=cn, oc=oc):
                tD = cn[:, oc * 4:oc * 4 + 4]
                acc = T["fsq"].ap()[:, 0:1024]
                e.tensor_scalar(acc, T["fx"].ap()[:, 0:1024], tD[:, 0:1],
                                None, ALU.mult)
                for i in (1, 2, 3):
                    e.scalar_tensor_tensor(acc, T["fx"].ap()[:, i:i + 1024],
                                           tD[:, i:i + 1], acc, ALU.mult, ALU.add)
                return e.engine_nop()
            sconv = P.add("vector", fconv, (sev, ldWsm), 1)
            ssil = P.add("scalar",
                         lambda e: e.activation(T["fhb"].ap()[:, 0:1024],
                                                T["fsq"].ap()[:, 0:1024], AF.Silu),
                         (sconv,), 1)

            if nm == "q":
                st = dma("scalq", T["qfb"][:, oc * TC:(oc + 1) * TC],
                         T["fhb"].ap()[:, 0:1024], (ssil,))
                proj_sp.append(st)
                prev_use = st
            if nm == "k":
                st = dma("scalq", T["kfb"][:, oc * TC:(oc + 1) * TC],
                         T["fhb"].ap()[:, 0:1024], (ssil,))
                proj_sp.append(st)
                prev_use = st
            if nm in ("k", "v", "w"):
                tr_evs = []
                for gidx in range(2):
                    pst = T["psT1"] if gidx == 0 else T["psT2"]
                    anti = prev_tr[gidx]

                    def fmt(e, pst=pst, gidx=gidx):
                        for k in range(4):
                            ttt = gidx * 4 + k
                            e.transpose(pst.ap()[:, k * 128:(k + 1) * 128],
                                        T["fhb"].ap()[:, ttt * 128:(ttt + 1) * 128],
                                        idn[:])
                        return e.nop()
                    smt = P.add("tensor", fmt, (ssil,) + ((anti,) if anti else ()), 1)

                    def fev2(e, pst=pst, gidx=gidx):
                        return e.tensor_copy(
                            T["tokst"].ap()[:, gidx * 512:gidx * 512 + 512],
                            pst.ap()[:, 0:512])
                    sev2 = P.add("vector", fev2, (smt,), 1)
                    prev_tr[gidx] = sev2
                    tr_evs.append(sev2)
                if nm == "w":
                    def fl2a(e):
                        wt = T["tokst"].ap()
                        e.tensor_tensor(T["fsq"].ap()[:, 0:1024], wt, wt, ALU.mult)
                        e.tensor_reduce(
                            T["tinyC"].ap()[:, 0:8],
                            T["fsq"].ap()[:, 0:1024].rearrange(
                                "p (tt dd) -> p tt dd", dd=128), AX.X, ALU.add)
                        return e.tensor_scalar(T["tinyC"].ap()[:, 0:8],
                                               T["tinyC"].ap()[:, 0:8], 1e-6,
                                               None, ALU.add)
                    sl2a = P.add("vector", fl2a, tuple(tr_evs), 1)
                    sl2b = P.add("scalar",
                                 lambda e: e.activation(T["tinyC"].ap()[:, 8:16],
                                                        T["tinyC"].ap()[:, 0:8],
                                                        AF.Sqrt), (sl2a,), 1)

                    def fl2c(e, oc=oc):
                        e.reciprocal(T["tinyC"].ap()[:, 16:24],
                                     T["tinyC"].ap()[:, 8:16])
                        wt = T["tokst"].ap()
                        for tt in range(8):
                            e.tensor_scalar(
                                wt[:, tt * 128:(tt + 1) * 128],
                                wt[:, tt * 128:(tt + 1) * 128],
                                T["tinyC"].ap()[:, 16 + tt:17 + tt],
                                T["beta_s"].ap()[:, tt * 16 + oc:tt * 16 + oc + 1],
                                ALU.mult, ALU.mult)
                        return e.engine_nop()
                    sl2c = P.add("vector", fl2c, (sl2b, beta_steps[-1]), 1)
                    tgt, depw = T["bwb"], (sl2c,)
                else:
                    tgt, depw = (T["ktokb"] if nm == "k" else T["vtokb"]), tuple(tr_evs)
                st = dma("scalq",
                         tgt.rearrange("(tt p) f -> p tt f", p=128)
                         [:, :, oc * 128:(oc + 1) * 128],
                         T["tokst"].ap().rearrange("p (tt dd) -> p tt dd", dd=128),
                         depw)
                proj_sp.append(st)
                prev_tr = [st, st]
                prev_use = st
    b1_done = proj_sp[-1]

    def chunk_loads(cidx, part2, deps):
        rows = slice(cidx * C, (cidx + 1) * C)
        pairs = [
            (a_ap(O_GHI, 0, C, D), T["ghb"][rows, :]),
            (a_ap(O_GLO, 0, C, D), T["glb"][rows, :]),
            (a_ap(O_KT, 0, C, D), T["ktokb"][rows, :]),
            (a_ap(O_VT, 0, C, D), T["vtokb"][rows, :]),
            (a_ap(O_BW, 0, C, D), T["bwb"][rows, :]),
        ]
        if part2:
            pairs.append((arC.ap()[:, O_QF:O_QF + 1024].rearrange(
                "p (h t) -> p h t", t=C),
                T["qfb"].rearrange("p (h t) -> p h t", t=TC)
                [:, :, cidx * C:(cidx + 1) * C]))
            pairs.append((arC.ap()[:, O_KF:O_KF + 1024].rearrange(
                "p (h t) -> p h t", t=C),
                T["kfb"].rearrange("p (h t) -> p h t", t=TC)
                [:, :, cidx * C:(cidx + 1) * C]))
        return mdma("sync", pairs, deps)

    def cum_mms(csrc, deps):
        def fmm(e, csrc=csrc):
            for nr in range(4):
                sl = slice(nr * 512, (nr + 1) * 512)
                e.matmul(T["psA"].ap()[0:C, sl], csrc[:],
                         a_ap(O_GHI, 0, C, D)[:, sl], start=True, stop=False)
                e.matmul(T["psA"].ap()[0:C, sl], csrc[:],
                         a_ap(O_GLO, 0, C, D)[:, sl], start=False, stop=True)
            return e.nop()
        return P.add("tensor", fmm, deps, 1)

    def dbc_steps(row_src_step, extra=()):
        def fmm(e):
            e.matmul(T["psB"].ap()[:, 0:512], ones1[:], drow[0:1, 0:512],
                     start=True, stop=True)
            e.matmul(T["psB"].ap()[:, 512:1024], ones1[:], drow[0:1, 512:1024],
                     start=True, stop=True)
            return e.nop()
        s1 = P.add("tensor", fmm, (row_src_step,) + tuple(extra), 1)
        s2 = P.add("vector", lambda e: e.tensor_copy(dbc[:, 0:1024],
                                                     T["psB"].ap()[:, 0:1024]),
                   (s1,), 1)

        def fmm2(e):
            e.matmul(T["psB"].ap()[:, 0:512], ones1[:], drow[0:1, 1024:1536],
                     start=True, stop=True)
            e.matmul(T["psB"].ap()[:, 512:1024], ones1[:], drow[0:1, 1536:2048],
                     start=True, stop=True)
            return e.nop()
        s3 = P.add("tensor", fmm2, (s2,), 1)
        s4 = P.add("vector", lambda e: e.tensor_copy(dbc[:, 1024:2048],
                                                     T["psB"].ap()[:, 0:1024]),
                   (s3,), 1)

        def fmm3(e):
            for h in range(16):
                e.matmul(T["psB"].ap()[:, h:h + 1],
                         drow[0:1, h * 128:(h + 1) * 128], ones1[0:1, 0:1],
                         start=True, stop=True)
            return e.nop()
        s5 = P.add("tensor", fmm3, (s4,), 1)
        s6 = P.add("vector", lambda e: e.tensor_copy(T["dvm"].ap()[:],
                                                     T["psB"].ap()[:, 0:16]),
                   (s5,), 1)
        return s6

    def f_lz(e):
        return e.memset(lacc, 0.0)
    s_lz = P.add("gpsimd", f_lz, (b1_done,), 1)

    def aend_steps(deps_psb):
        onecol = T["csA_s"][:, 63:64]

        def fa1(e):
            for nr in (0, 1):
                sl = slice(nr * 512, (nr + 1) * 512)
                e.matmul(T["psB"].ap()[0:1, sl], onecol,
                         a_ap(O_GHI, 0, C, D)[:, sl], start=True, stop=False)
                e.matmul(T["psB"].ap()[0:1, sl], onecol,
                         a_ap(O_GLO, 0, C, D)[:, sl], start=False, stop=True)
            return e.nop()
        sa1 = P.add("tensor", fa1, deps_psb, 1)
        sv1 = P.add("vector",
                    lambda e: e.tensor_copy(srow[0:1, 0:1024],
                                            T["psB"].ap()[0:1, 0:1024]), (sa1,), 1)

        def fa2(e):
            for nr in (2, 3):
                sl0 = slice((nr - 2) * 512, (nr - 1) * 512)
                sl = slice(nr * 512, (nr + 1) * 512)
                e.matmul(T["psB"].ap()[0:1, sl0], onecol,
                         a_ap(O_GHI, 0, C, D)[:, sl], start=True, stop=False)
                e.matmul(T["psB"].ap()[0:1, sl0], onecol,
                         a_ap(O_GLO, 0, C, D)[:, sl], start=False, stop=True)
            return e.nop()
        sa2 = P.add("tensor", fa2, (sv1,), 1)
        sv2 = P.add("vector",
                    lambda e: e.tensor_copy(srow[0:1, 1024:2048],
                                            T["psB"].ap()[0:1, 0:1024]), (sa2,), 1)
        return sv2

    prev_chunk = [b1_done, s_lz]
    for cidx in range(NCH):
        ldc = chunk_loads(cidx, False, tuple(prev_chunk))
        sA = cum_mms(T["csA_s"], (ldc,))

        def f_exps(e):
            return e.activation(scf, T["psA"].ap()[0:C, :], AF.Exp, scale=-1.0)
        s_ex = P.add("scalar", f_exps, (sA,), 1)
        s_u = P.add("vector",
                    lambda e: e.tensor_tensor(a_ap(O_U, 0, C, D),
                                              a_ap(O_BW, 0, C, D), scf, ALU.mult),
                    (s_ex,), 1)
        s_ae = aend_steps((ldc,) + tuple(prev_chunk))

        def f_mv(e):
            e.dma_start(out=lsc, in_=srow).then_inc(sems["qd"], 16)
            return None
        s_rm = P.add("gpsimdd", f_mv, (s_ae,), 16, is_dma=True)
        s_la = P.add("vector",
                     lambda e: e.tensor_tensor(lacc, lacc, lsc, ALU.add),
                     (s_rm,), 1)
        s_dr = P.add("scalar",
                     lambda e: e.activation(drow, srow, AF.Exp), (s_ae,), 1)
        sS = cum_mms(T["csS_s"], (s_u, s_ex))
        s_ex2 = P.add("scalar",
                      lambda e: e.activation(scf, T["psA"].ap()[0:C, :], AF.Exp),
                      (sS,), 1)
        s_vu = P.add("vector",
                     lambda e: e.tensor_tensor(a_ap(O_VU, 0, C, D),
                                               a_ap(O_BW, 0, C, D), scf, ALU.mult),
                     (s_ex2,), 1)
        dbc_done = dbc_steps(s_dr)

        def f_updk(e):
            for h in range(16):
                e.matmul(T["psA"].ap()[:, h * 128:(h + 1) * 128],
                         a_ap(O_KT, 0, C, D)[:, h * 128:(h + 1) * 128],
                         a_ap(O_VU, 0, C, D)[:, h * 128:(h + 1) * 128],
                         start=True, stop=True)
            return e.nop()
        s_uk = P.add("tensor", f_updk, (s_vu, s_ex2), 1)

        def f_bk(e):
            e.tensor_tensor(T["fx2"].ap()[:], T["SkS"].ap()[:], dbc, ALU.mult)
            return e.tensor_tensor(T["SkS"].ap()[:], T["fx2"].ap()[:],
                                   T["psA"].ap()[:], ALU.add)
        s_bk = P.add("vector", f_bk, (s_uk, dbc_done), 1)

        def f_updv(e):
            for h in range(16):
                e.matmul(T["psA"].ap()[:, h * 128:(h + 1) * 128],
                         a_ap(O_VU, 0, C, D)[:, h * 128:(h + 1) * 128],
                         a_ap(O_VT, 0, C, D)[:, h * 128:(h + 1) * 128],
                         start=True, stop=True)
            return e.nop()
        s_uv = P.add("tensor", f_updv, (s_bk,), 1)

        def f_bv(e):
            e.tensor_tensor(
                T["fx2"].ap().rearrange("p (h dd) -> p h dd", dd=128),
                T["SvS"].ap().rearrange("p (h dd) -> p h dd", dd=128),
                T["dvm"].ap().unsqueeze(2).broadcast_to([128, 16, 128]), ALU.mult)
            return e.tensor_tensor(T["SvS"].ap()[:], T["fx2"].ap()[:],
                                   T["psA"].ap()[:], ALU.add)
        s_bv = P.add("vector", f_bv, (s_uv, dbc_done), 1)
        prev_chunk = [s_bv, s_uv, s_rm, s_la, s_dr]

    pub = mdma("scalq", [
        (T["cc_in"][0:131072].bitcast(BF16).rearrange("(p f) -> p f", p=128),
         T["SkS"].ap()[:]),
        (T["cc_in"][131072:262144].bitcast(BF16).rearrange("(p f) -> p f", p=128),
         T["SvS"].ap()[:]),
        (T["cc_in"][262144:264192].bitcast(F32).rearrange("(p f) -> p f", p=16),
         lacc),
    ], tuple(prev_chunk))

    def f_cc(e):
        e.collective_compute(
            "AllGather", ALU.bypass,
            replica_groups=[list(range(NCORES))],
            ins=[T["cc_in"]], outs=[T["cc_out"]],
        ).then_inc(sems["qc"], 1)
        return None
    s_cc = P.add("gpsimdc", f_cc, (pub,), 1, is_dma=True)

    s_sz = P.add("gpsimd", lambda e: (e.memset(T["SkS"][:], 0.0),
                                      e.memset(T["SvS"][:], 0.0))[1],
                 (pub,), 1)

    CCW = T["CCW"]
    fold_last = s_cc
    for j in range(7):
        ldb = mdma("sync", [
            (a_ap(O_BJ, 0, 128, D),
             T["cc_out"][j * CCW:j * CCW + 131072].bitcast(BF16).rearrange(
                 "(p f) -> p f", p=128)),
            (a_ap(O_BJV, 0, 128, D),
             T["cc_out"][j * CCW + 131072:j * CCW + 262144].bitcast(BF16)
             .rearrange("(p f) -> p f", p=128)),
            (srow,
             T["cc_out"][j * CCW + 262144:j * CCW + 264192].bitcast(F32)
             .rearrange("(p f) -> p f", p=1)),
        ], (fold_last, s_sz))
        s_m = P.add("vector",
                    lambda e, j=j: e.tensor_scalar(
                        srow, srow, T["m7_s"].ap()[0:1, j:j + 1], None, ALU.mult),
                    (ldb,), 1)
        s_d = P.add("scalar",
                    lambda e: e.activation(drow, srow, AF.Exp), (s_m,), 1)
        dbcj = dbc_steps(s_d)

        def f_fk(e, j=j):
            e.tensor_tensor(T["fx2"].ap()[:], T["SkS"].ap()[:], dbc, ALU.mult)
            return e.scalar_tensor_tensor(
                T["SkS"].ap()[:], a_ap(O_BJ, 0, 128, D),
                T["m7b_s"].ap()[:, j:j + 1], T["fx2"].ap()[:],
                ALU.mult, ALU.add)
        s_fk = P.add("vector", f_fk, (dbcj, ldb, s_sz), 1)

        def f_fv(e, j=j):
            e.tensor_tensor(
                T["fx2"].ap().rearrange("p (h dd) -> p h dd", dd=128),
                T["SvS"].ap().rearrange("p (h dd) -> p h dd", dd=128),
                T["dvm"].ap().unsqueeze(2).broadcast_to([128, 16, 128]), ALU.mult)
            return e.scalar_tensor_tensor(
                T["SvS"].ap()[:], a_ap(O_BJV, 0, 128, D),
                T["m7b_s"].ap()[:, j:j + 1], T["fx2"].ap()[:],
                ALU.mult, ALU.add)
        s_fv = P.add("vector", f_fv, (s_fk, dbcj), 1)
        fold_last = s_fv

    prev_chunk = [fold_last]
    o_spills = []
    for cidx in range(NCH):
        ldc = chunk_loads(cidx, True, tuple(prev_chunk))
        sA = cum_mms(T["csA_s"], (ldc,))

        def f_exps(e):
            e.activation(a_ap(O_EXPA, 0, C, D), T["psA"].ap()[0:C, :], AF.Exp)
            return e.activation(scf, T["psA"].ap()[0:C, :], AF.Exp, scale=-1.0)
        s_ex = P.add("scalar", f_exps, (sA,), 1)
        s_u = P.add("vector",
                    lambda e: e.tensor_tensor(a_ap(O_U, 0, C, D),
                                              a_ap(O_BW, 0, C, D), scf, ALU.mult),
                    (s_ex,), 1)
        s_ae = aend_steps((ldc,) + tuple(prev_chunk))
        s_rm = s_ae
        s_dr = P.add("scalar",
                     lambda e: e.activation(drow, srow, AF.Exp), (s_ae,), 1)
        sS = cum_mms(T["csS_s"], (s_u, s_ex))
        s_ex2 = P.add("scalar",
                      lambda e: e.activation(scf, T["psA"].ap()[0:C, :], AF.Exp),
                      (sS,), 1)
        s_vu = P.add("vector",
                     lambda e: e.tensor_tensor(a_ap(O_VU, 0, C, D),
                                               a_ap(O_BW, 0, C, D), scf, ALU.mult),
                     (s_ex2,), 1)

        def f_qk(e):
            for h in range(16):
                e.matmul(T["psB"].ap()[0:C, h * C:(h + 1) * C],
                         arC.ap()[:, O_KF + h * C:O_KF + (h + 1) * C],
                         arC.ap()[:, O_QF + h * C:O_QF + (h + 1) * C],
                         start=True, stop=True)
            return e.nop()
        s_qk = P.add("tensor", f_qk, (ldc, s_ae), 1)

        def f_qkb(e):
            return e.tensor_tensor(
                a_ap(O_QKB, 0, C, 1024).rearrange("p (h t) -> p h t", t=C),
                T["psB"].ap()[0:C, 0:1024].rearrange("p (h t) -> p h t", t=C),
                T["m64_s"].ap().unsqueeze(1).broadcast_to([C, 16, C]), ALU.mult)
        s_qkb = P.add("vector", f_qkb, (s_qk,), 1)

        def f_t1(e):
            for h in range(16):
                e.matmul(T["psA"].ap()[0:C, h * 128:(h + 1) * 128],
                         a_ap(O_QKB, 0, C, 1024)[:, h * C:(h + 1) * C],
                         a_ap(O_U, 0, C, D)[:, h * 128:(h + 1) * 128],
                         start=True, stop=False)
                e.matmul(T["psA"].ap()[0:C, h * 128:(h + 1) * 128],
                         arC.ap()[:, O_QF + h * C:O_QF + (h + 1) * C],
                         T["SkS"].ap()[:, h * 128:(h + 1) * 128],
                         start=False, stop=True)
            return e.nop()
        s_t1 = P.add("tensor", f_t1, (s_qkb, s_u, s_ex2, s_vu), 1)

        def f_sc(e):
            return e.scalar_tensor_tensor(scf, T["psA"].ap()[0:C, :],
                                          ATT_SCALE, a_ap(O_EXPA, 0, C, D),
                                          ALU.mult, ALU.mult)
        s_sc = P.add("vector", f_sc, (s_t1,), 1)

        def f_mx(e):
            e.tensor_reduce(T["tinyA"].ap()[0:C, 0:16],
                            scf.rearrange("p (h m) -> p h m", m=128),
                            AX.X, ALU.max)
            return e.tensor_tensor(
                scf.rearrange("p (h m) -> p h m", m=128),
                scf.rearrange("p (h m) -> p h m", m=128),
                T["tinyA"].ap()[0:C, 0:16].unsqueeze(2).broadcast_to([C, 16, 128]),
                ALU.subtract)
        s_mx = P.add("vector", f_mx, (s_sc,), 1)
        s_e2 = P.add("scalar",
                     lambda e: e.activation(scf, scf, AF.Exp), (s_mx,), 1)

        def f_sm(e):
            e.tensor_reduce(T["tinyA"].ap()[0:C, 16:32],
                            scf.rearrange("p (h m) -> p h m", m=128),
                            AX.X, ALU.add)
            e.reciprocal(T["tinyA"].ap()[0:C, 32:48], T["tinyA"].ap()[0:C, 16:32])
            e.tensor_tensor(
                scf.rearrange("p (h m) -> p h m", m=128),
                scf.rearrange("p (h m) -> p h m", m=128),
                T["tinyA"].ap()[0:C, 32:48].unsqueeze(2).broadcast_to([C, 16, 128]),
                ALU.mult)
            return e.tensor_tensor(a_ap(O_ACHT, 0, C, D), scf,
                                   a_ap(O_EXPA, 0, C, D), ALU.mult)
        s_sm = P.add("vector", f_sm, (s_e2,), 1)

        def f_at(e):
            for h in range(16):
                e.transpose(T["psT1"].ap()[:, h * C:(h + 1) * C],
                            a_ap(O_ACHT, 0, C, D)[:, h * 128:(h + 1) * 128],
                            idn[0:C, 0:C])
            return e.nop()
        s_at = P.add("tensor", f_at, (s_sm,), 1)
        s_atv = P.add("vector",
                      lambda e: e.tensor_copy(arC.ap()[:, O_ATB:O_ATB + 1024],
                                              T["psT1"].ap()[:]), (s_at,), 1)

        def f_ut(e):
            for h in range(16):
                e.transpose(T["psT2"].ap()[:, h * C:(h + 1) * C],
                            a_ap(O_U, 0, C, D)[:, h * 128:(h + 1) * 128],
                            idn[0:C, 0:C])
            return e.nop()
        s_ut = P.add("tensor", f_ut, (s_u, s_atv), 1)
        s_utv = P.add("vector",
                      lambda e: e.tensor_copy(arC.ap()[:, O_UTB:O_UTB + 1024],
                                              T["psT2"].ap()[:]), (s_ut,), 1)

        def f_pt(e):
            for h in range(16):
                e.matmul(T["psB"].ap()[0:C, h * C:(h + 1) * C],
                         arC.ap()[:, O_UTB + h * C:O_UTB + (h + 1) * C],
                         arC.ap()[:, O_ATB + h * C:O_ATB + (h + 1) * C],
                         start=True, stop=True)
            return e.nop()
        s_pt = P.add("tensor", f_pt, (s_utv, s_qkb), 1)

        def f_pb(e):
            return e.tensor_tensor(
                a_ap(O_QKB, 0, C, 1024).rearrange("p (h t) -> p h t", t=C),
                T["psB"].ap()[0:C, 0:1024].rearrange("p (h t) -> p h t", t=C),
                T["m64_s"].ap().unsqueeze(1).broadcast_to([C, 16, C]), ALU.mult)
        s_pb = P.add("vector", f_pb, (s_pt,), 1)

        def f_o(e):
            for h in range(16):
                e.matmul(T["psA"].ap()[0:C, h * 128:(h + 1) * 128],
                         a_ap(O_QKB, 0, C, 1024)[:, h * C:(h + 1) * C],
                         a_ap(O_VT, 0, C, D)[:, h * 128:(h + 1) * 128],
                         start=True, stop=False)
                e.matmul(T["psA"].ap()[0:C, h * 128:(h + 1) * 128],
                         arC.ap()[:, O_ATB + h * C:O_ATB + (h + 1) * C],
                         T["SvS"].ap()[:, h * 128:(h + 1) * 128],
                         start=False, stop=True)
            return e.nop()
        s_o = P.add("tensor", f_o, (s_pb, s_sc), 1)
        s_oe = P.add("vector",
                     lambda e: e.tensor_copy(a_ap(O_OEV, 0, C, D),
                                             T["psA"].ap()[0:C, :]), (s_o,), 1)
        sp = dma("scalq", T["ob"][cidx * C:(cidx + 1) * C, :], a_ap(O_OEV, 0, C, D),
                 (s_oe,))
        o_spills.append(sp)

        dbc_done = dbc_steps(s_dr, extra=(s_pb,))

        def f_updk(e):
            for h in range(16):
                e.matmul(T["psA"].ap()[:, h * 128:(h + 1) * 128],
                         a_ap(O_KT, 0, C, D)[:, h * 128:(h + 1) * 128],
                         a_ap(O_VU, 0, C, D)[:, h * 128:(h + 1) * 128],
                         start=True, stop=True)
            return e.nop()
        s_uk = P.add("tensor", f_updk, (s_oe,), 1)

        def f_sk(e):
            e.tensor_tensor(T["fx2"].ap()[:], T["SkS"].ap()[:], dbc, ALU.mult)
            return e.tensor_tensor(T["SkS"].ap()[:], T["fx2"].ap()[:],
                                   T["psA"].ap()[:], ALU.add)
        s_sk = P.add("vector", f_sk, (s_uk, dbc_done), 1)

        def f_updv(e):
            for h in range(16):
                e.matmul(T["psA"].ap()[:, h * 128:(h + 1) * 128],
                         a_ap(O_VU, 0, C, D)[:, h * 128:(h + 1) * 128],
                         a_ap(O_VT, 0, C, D)[:, h * 128:(h + 1) * 128],
                         start=True, stop=True)
            return e.nop()
        s_uv = P.add("tensor", f_updv, (s_sk,), 1)

        def f_sv(e):
            e.tensor_tensor(
                T["fx2"].ap().rearrange("p (h dd) -> p h dd", dd=128),
                T["SvS"].ap().rearrange("p (h dd) -> p h dd", dd=128),
                T["dvm"].ap().unsqueeze(2).broadcast_to([128, 16, 128]), ALU.mult)
            return e.tensor_tensor(T["SvS"].ap()[:], T["fx2"].ap()[:],
                                   T["psA"].ap()[:], ALU.add)
        s_sv = P.add("vector", f_sv, (s_uv, dbc_done), 1)
        prev_chunk = [s_sv, s_uv, s_dr, sp]

    ldWo = dma("gpsimdd", arC.ap()[:, 0:32768].rearrange("p (c f) -> p c f", f=2048),
               T["Wo"].rearrange("(c p) f -> p c f", p=128),
               tuple(prev_chunk) + (o_spills[-1],))
    x2outs = []
    prev_t = None
    prev_ps = None
    prev_bg = None
    for tt in range(8):
        rows = slice(tt * 128, (tt + 1) * 128)
        ldo = dma("sync", T["fhb"].ap()[:], T["ob"][rows, :],
                  (prev_t,) if prev_t else (o_spills[-1],))
        ldbg = mdma("sync", [
            (T["rowslot"].ap()[0:1, 0:512], T["rowsb"][2:3, 2048:2560]),
            (T["rowslot"].ap()[0:1, 512:1024], T["rowsb"][3:4, 2048:2560]),
        ], (prev_bg,) if prev_bg else (s_bg,) + tuple(prev_chunk)) if tt == 0 else None
        if tt == 0:
            more = []
            lb = ldbg
            for piece in range(1, 4):
                lb = mdma("sync", [
                    (T["fx"].ap().bitcast(BF16)[0:1, piece * 512:piece * 512 + 512],
                     T["rowsb"][2:3, 2048 + piece * 512:2048 + (piece + 1) * 512]),
                    (T["tokst"].ap()[0:1, (piece - 1) * 341:(piece - 1) * 341 + 512]
                     if False else
                     T["tokst"].ap()[0:1, 0:512] if piece == 1 else
                     T["tokst"].ap()[0:1, 512:1024] if piece == 2 else
                     T["sm1"].ap()[0:1, 0:512],
                     T["rowsb"][3:4, 2048 + piece * 512:2048 + (piece + 1) * 512]),
                ], (lb,))
                more.append(lb)
            bg_ld = more[-1]

        def bg_hi_ap(nr):
            if nr == 0:
                return T["rowslot"].ap()[0:1, 0:512]
            return T["fx"].ap().bitcast(BF16)[0:1, nr * 512:nr * 512 + 512]

        def bg_lo_ap(nr):
            if nr == 0:
                return T["rowslot"].ap()[0:1, 512:1024]
            if nr == 1:
                return T["tokst"].ap()[0:1, 0:512]
            if nr == 2:
                return T["tokst"].ap()[0:1, 512:1024]
            return T["sm1"].ap()[0:1, 0:512]

        def f_gmm(e, tt=tt):
            for nr in range(4):
                sl = slice(nr * 512, (nr + 1) * 512)
                e.matmul(T["psA"].ap()[:, sl], hg1[:, 3 + tt * 128:3 + (tt + 1) * 128],
                         Wg2b[:, sl], start=True, stop=False)
                e.matmul(T["psA"].ap()[:, sl], ones1[:], bg_hi_ap(nr),
                         start=False, stop=False)
                e.matmul(T["psA"].ap()[:, sl], ones1[:], bg_lo_ap(nr),
                         start=False, stop=True)
            return e.nop()
        s_gm = P.add("tensor", f_gmm,
                     ((prev_ps,) if prev_ps else tuple(prev_chunk) + (bg_ld,)), 1)
        s_sg = P.add("scalar",
                     lambda e: e.activation(T["qh"].ap()[:], T["psA"].ap()[:],
                                            AF.Sigmoid), (s_gm,), 1)
        s_q2 = P.add("scalar",
                     lambda e: e.activation(T["fsq"].ap()[:], T["fhb"].ap()[:],
                                            AF.Square), (ldo,), 1)

        def f_rr(e):
            e.tensor_reduce(T["tinyA"].ap()[:, 0:16],
                            T["fsq"].ap().rearrange("p (h m) -> p h m", m=128),
                            AX.X, ALU.add)
            return e.tensor_scalar(T["tinyA"].ap()[:, 0:16],
                                   T["tinyA"].ap()[:, 0:16], 1.0 / 128.0, 1e-5,
                                   ALU.mult, ALU.add)
        s_rr = P.add("vector", f_rr, (s_q2,), 1)
        s_rs = P.add("scalar",
                     lambda e: e.activation(T["tinyA"].ap()[:, 16:32],
                                            T["tinyA"].ap()[:, 0:16], AF.Sqrt),
                     (s_rr,), 1)

        def f_gate(e):
            e.reciprocal(T["tinyA"].ap()[:, 32:48], T["tinyA"].ap()[:, 16:32])
            e.tensor_tensor(
                T["fsq"].ap().rearrange("p (h m) -> p h m", m=128),
                T["fhb"].ap().rearrange("p (h m) -> p h m", m=128),
                T["tinyA"].ap()[:, 32:48].unsqueeze(2).broadcast_to([128, 16, 128]),
                ALU.mult)
            e.tensor_tensor(
                T["fsq"].ap().rearrange("p (h m) -> p h m", m=128),
                T["fsq"].ap().rearrange("p (h m) -> p h m", m=128),
                T["nw_bc"].ap().unsqueeze(1).broadcast_to([128, 16, 128]),
                ALU.mult)
            return e.tensor_tensor(T["fhb"].ap()[:], T["fsq"].ap()[:],
                                   T["qh"].ap()[:], ALU.mult)
        s_g8 = P.add("vector", f_gate, (s_rs, s_sg), 1)
        tr_evs = []
        for gidx in range(2):
            pst = T["psT1"] if gidx == 0 else T["psT2"]

            def fmt(e, pst=pst, gidx=gidx):
                for k in range(8):
                    occ = gidx * 8 + k
                    e.transpose(pst.ap()[:, k * 128:(k + 1) * 128],
                                T["fhb"].ap()[:, occ * 128:(occ + 1) * 128],
                                idn[:])
                return e.nop()
            smt = P.add("tensor", fmt, (s_g8,) + ((prev_t,) if prev_t else ()), 1)

            def fev(e, pst=pst, gidx=gidx):
                return e.tensor_copy(
                    T["qh"].ap()[:, gidx * 1024:(gidx + 1) * 1024], pst.ap()[:])
            sev = P.add("vector", fev, (smt,), 1)
            tr_evs.append(sev)

        def f_wo(e):
            for occ in range(16):
                for nr in range(4):
                    e.matmul(T["psA"].ap()[:, nr * 512:(nr + 1) * 512],
                             T["qh"].ap()[:, occ * 128:(occ + 1) * 128],
                             arC.ap()[:, occ * 2048 + nr * 512:occ * 2048 + (nr + 1) * 512],
                             start=(occ == 0), stop=(occ == 15))
            return e.nop()
        s_wo = P.add("tensor", f_wo, tuple(tr_evs) + (ldWo, s_sg), 1)
        ldx1 = dma("sync", T["fx2"].ap()[:],
                   T["x1b"][3 + tt * 128:3 + (tt + 1) * 128, :], (ldo, s_gm))
        s_x2 = P.add("vector",
                     lambda e: e.tensor_tensor(T["fx2"].ap()[:], T["fx2"].ap()[:],
                                               T["psA"].ap()[:], ALU.add),
                     (s_wo, ldx1), 1)
        prev_ps = s_x2
        stx = dma("scalq", T["x2b"][rows, :], T["fx2"].ap()[:], (s_x2,))
        x2outs.append(stx)
        prev_t = stx

    final_outs = ffn(2, tuple(x2outs[-2:]))
    P.add("sync", lambda e: None, (final_outs[-1],), 1)


def kernel(**inputs):
    x = np.asarray(inputs["L0_attn_hidden_states"], np.float32)[0]
    n = NCORES

    def f32(a):
        return np.ascontiguousarray(np.asarray(a, np.float32))

    common = {
        "Wq": f32(inputs["L0_attn_Wq"]), "Wk": f32(inputs["L0_attn_Wk"]),
        "Wv": f32(inputs["L0_attn_Wv"]), "Ww": f32(inputs["L0_attn_Ww"]),
        "Wo": f32(inputs["L0_attn_Wo"]),
        "cq": f32(inputs["L0_attn_cq"]), "ck": f32(inputs["L0_attn_ck"]),
        "cv": f32(inputs["L0_attn_cv"]),
        "Wf1": f32(inputs["L0_attn_Wf1"]), "Wf2": f32(inputs["L0_attn_Wf2"]),
        "Wb": f32(inputs["L0_attn_Wb"]),
        "Wg1": f32(inputs["L0_attn_Wg1"]), "Wg2": f32(inputs["L0_attn_Wg2"]),
        "bg2": f32(inputs["L0_attn_bg2"]).reshape(1, D),
        "norm_w": f32(inputs["L0_attn_norm_w"]).reshape(1, 128),
        "lnw": f32(inputs["L0_ffn_ln_w"]).reshape(1, D),
        "lnb": f32(inputs["L0_ffn_ln_b"]).reshape(1, D),
        "W1": f32(inputs["L0_ffn_W1"]),
        "b1": f32(inputs["L0_ffn_b1"]).reshape(1, DH),
        "W2": f32(inputs["L0_ffn_W2"]),
        "b2": f32(inputs["L0_ffn_b2"]).reshape(1, D),
        "idn": np.eye(128, dtype=BF),
        "ones1": np.ones((1, 128), BF),
        "csA": np.triu(np.ones((C, C), np.float32)).astype(BF),
        "csS": np.tril(np.ones((C, C), np.float32), -1).astype(BF),
        "mask64": np.triu(np.ones((C, C), np.float32)).astype(BF),
    }
    in_maps = []
    for i in range(n):
        lo = i * TC
        if i == 0:
            xs = np.concatenate([np.zeros((3, D), np.float32), x[0:TC]], 0)
        else:
            xs = x[lo - 3:lo + TC]
        m = dict(common)
        m["x_in"] = np.ascontiguousarray(xs)
        m["pfx"] = (np.zeros if i == 0 else np.ones)((128, 3), np.float32)
        mk = np.zeros((1, 7), np.float32)
        mk[0, :i] = 1.0
        m["mask7"] = mk
        m["mask7b"] = np.repeat(mk, 128, axis=0)
        in_maps.append(m)

    nc = build_program()
    res = run_bass_kernel_spmd(nc, in_maps, core_ids=list(range(n)))
    out = np.concatenate([np.asarray(r["out"]) for r in res.results], axis=0)
    return out[None].astype(np.float32)
